# version 01
# baseline (speedup 1.0000x reference)
import numpy as np
import concourse.bass as bass
import concourse.mybir as mybir
from concourse.bass_utils import run_bass_kernel_spmd
from concourse.alu_op_type import AluOpType as OP

F32 = mybir.dt.float32
F32R = mybir.dt.float32r
F16 = mybir.dt.float16
ACT = mybir.ActivationFunctionType
AX = mybir.AxisListType

B, S, D, H, E = 4, 2048, 2048, 16, 128
I = 5632
NK = D // 128
NSLOT = 8
QL = NSLOT * 128
NIC = I // 128
INV_ATTN = float(np.float32(1.0) / np.float32(np.sqrt(np.float32(E))))
MASK_NEG = -1.0e9
INV_D = float(np.float32(1.0) / np.float32(D))
LN_EPS = 1e-5
RMS_EPS = 1e-6

ENGINES = ("sync", "tensor", "vector", "scalar", "gpsimd")
FFN_BLOCKS = [list(range(0, 16)), list(range(16, 32)), list(range(32, 44))]


def build_program(debug=0):
    nc = bass.Bass("TRN2", target_bir_lowering=False)

    xT = nc.dram_tensor("xT", [NK, 128, S], F32R, kind="ExternalInput").ap()
    xqf = nc.dram_tensor("xqf", [NK, 128, QL], F32R, kind="ExternalInput").ap()
    xq = nc.dram_tensor("xq", [QL, D], F32R, kind="ExternalInput").ap()
    wq = nc.dram_tensor("wq", [H, 128, NK * 128], F32R, kind="ExternalInput").ap()
    wk = nc.dram_tensor("wk", [H, 128, NK * 128], F32R, kind="ExternalInput").ap()
    wvr = nc.dram_tensor("wvr", [NK, 128, D], F32R, kind="ExternalInput").ap()
    wo = nc.dram_tensor("wo", [NK, 128, H * 128], F32R, kind="ExternalInput").ap()
    bqk = nc.dram_tensor("bqk", [128, 2 * H], F32, kind="ExternalInput").ap()
    bvb = nc.dram_tensor("bvb", [128, D], F32, kind="ExternalInput").ap()
    bo = nc.dram_tensor("bo", [128, NK], F32, kind="ExternalInput").ap()
    maskarr = nc.dram_tensor("maskarr", [128, NSLOT * 256], F32, kind="ExternalInput").ap()
    identm = nc.dram_tensor("identm", [128, 128], F32R, kind="ExternalInput").ap()
    w4gu = nc.dram_tensor("w4gu", [2 * NIC, 128, NK * 128], F16, kind="ExternalInput").ap()
    sbgu = nc.dram_tensor("sbgu", [2 * NIC, 128, NK * 128], F16, kind="ExternalInput").ap()
    w4d = nc.dram_tensor("w4d", [NK, 128, NIC * 128], F16, kind="ExternalInput").ap()
    sbd = nc.dram_tensor("sbd", [NK, 128, NIC * 128], F16, kind="ExternalInput").ap()

    out = nc.dram_tensor("out", [QL, D], F32R, kind="ExternalOutput").ap()

    k_hbm = nc.dram_tensor("k_hbm", [H, 128, S], F32R).ap()
    v_hbm = nc.dram_tensor("v_hbm", [H, S, 128], F32R).ap()
    q_hbm = nc.dram_tensor("q_hbm", [H, 128, QL], F32R).ap()
    x1t_hbm = nc.dram_tensor("x1t_hbm", [QL, D], F32R).ap()
    x1f_hbm = nc.dram_tensor("x1f_hbm", [NSLOT, 128, NK * 128], F32R).ap()
    y_hbm = nc.dram_tensor("y_hbm", [NK, 128, QL], F32R).ap()

    dbg = {}
    if debug:
        dbg_k = nc.dram_tensor("dbg_k", [H, 128, S], F32R, kind="ExternalOutput").ap()
        dbg_v = nc.dram_tensor("dbg_v", [H, S, 128], F32R, kind="ExternalOutput").ap()
        dbg_q = nc.dram_tensor("dbg_q", [H, 128, QL], F32R, kind="ExternalOutput").ap()
        dbg_x1 = nc.dram_tensor("dbg_x1", [QL, D], F32R, kind="ExternalOutput").ap()
        dbg_z = nc.dram_tensor("dbg_z", [2, 128, H * 512], F32R, kind="ExternalOutput").ap()
        dbg_af = nc.dram_tensor("dbg_af", [2, 128, NK * 512], F32R, kind="ExternalOutput").ap()
        dbg_t0 = nc.dram_tensor("dbg_t0", [QL, D], F32R, kind="ExternalOutput").ap()
        dbg_st = nc.dram_tensor("dbg_st", [NSLOT, 128, 64], F32, kind="ExternalOutput").ap()
        dbg_ln = nc.dram_tensor("dbg_ln", [4, 128, D], F32R, kind="ExternalOutput").ap()
        dbg_tail = nc.dram_tensor("dbg_tail", [4, 128, D], F32R, kind="ExternalOutput").ap()
        dbg_y = nc.dram_tensor("dbg_y", [NK, 128, QL], F32R, kind="ExternalOutput").ap()

    ops = {e: [] for e in ENGINES}
    cnt = {}
    sems = {}

    def inc(sem, n):
        cnt[sem] = cnt.get(sem, 0) + n
        return cnt[sem]

    def emit(eng, fn, then=None):
        if then is None:
            ops[eng].append(fn)
            return None
        sem_name, n = then
        v = inc(sem_name, n)

        def wrapped(e, fn=fn, sem_name=sem_name, n=n):
            fn(e).then_inc(sems[sem_name], n)
        ops[eng].append(wrapped)
        return (sem_name, v)

    def wait(eng, guard):
        if guard is None:
            return
        sem, v = guard
        if v > 0:
            ops[eng].append(lambda e, s=sem, v=v: e.wait_ge(sems[s], v))

    def gmax(*guards):
        gs = [g for g in guards if g is not None and g[1] > 0]
        if not gs:
            return None
        sem = gs[0][0]
        assert all(g[0] == sem for g in gs), gs
        return (sem, max(g[1] for g in gs))

    def dma(qeng, dst, src, sem="ld", accum=None):
        if accum is None:
            return emit(qeng, lambda e, d=dst, s=src: e.dma_start(out=d, in_=s), (sem, 16))
        return emit(qeng, lambda e, d=dst, s=src, a=accum:
                    e.dma_start(out=d, in_=s, accum_op=a), (sem, 16))

    from contextlib import ExitStack
    es = ExitStack()
    with es:
        A = es.enter_context(nc.sbuf_tensor("A", [128, 32768], F32R))
        Bb = es.enter_context(nc.sbuf_tensor("Bb", [128, 11264], F32R))
        C = es.enter_context(nc.sbuf_tensor("C", [128, 8192], F16))
        t_ident = es.enter_context(nc.sbuf_tensor("t_ident", [128, 128], F32R))
        t_mask = es.enter_context(nc.sbuf_tensor("t_mask", [128, NSLOT * 256], F32))
        t_bqk = es.enter_context(nc.sbuf_tensor("t_bqk", [128, 2 * H], F32))
        t_bvb = es.enter_context(nc.sbuf_tensor("t_bvb", [128, D], F32))
        t_bo = es.enter_context(nc.sbuf_tensor("t_bo", [128, NK], F32))
        t_st = es.enter_context(nc.sbuf_tensor("t_st", [128, 64], F32))

        pS = es.enter_context(nc.psum_tensor("pS", [128, 2048], F32))
        pT = es.enter_context(nc.psum_tensor("pT", [128, 512], F32))
        pT2 = es.enter_context(nc.psum_tensor("pT2", [128, 512], F32))
        pZ = es.enter_context(nc.psum_tensor("pZ", [128, 512], F32))
        pW = es.enter_context(nc.psum_tensor("pW", [128, 512], F32))

        for s in ["ld", "sp", "gpd", "pe", "dv", "ac", "gp"]:
            sems[s] = es.enter_context(nc.semaphore(s))

        def matmul(out_ap, lhsT_ap, rhs_ap, first, last, then=None):
            return emit("tensor",
                        lambda e, o=out_ap, l=lhsT_ap, r=rhs_ap, f=first, la=last:
                        e.matmul(o, l.bitcast(F32R), r.bitcast(F32R), start=f, stop=la),
                        then)

        def transpose(out_ap, in_ap, then=None):
            return emit("tensor",
                        lambda e, o=out_ap, i=in_ap:
                        e.transpose(o.bitcast(F32R), i.bitcast(F32R), t_ident[:, :]),
                        then)

        g_const = None
        for dst, src in [(t_ident[:, :], identm), (t_mask[:, :], maskarr),
                         (t_bqk[:, :], bqk), (t_bvb[:, :], bvb), (t_bo[:, :], bo)]:
            g_const = dma("sync", dst, src)

        g_wvr = None
        for k in range(NK):
            g_wvr = dma("sync", A[:, k * 2048:(k + 1) * 2048], wvr[k])
        xcol_loaded = [None, None]
        xcol_free = [None, None]
        vstg_busy = None
        pS_free = [None, None, None, None]
        for tc in range(S // 128):
            cb = tc % 2
            wait("sync", xcol_free[cb])
            for k in range(NK):
                xcol_loaded[cb] = dma("sync",
                                      Bb[:, cb * 2048 + k * 128: cb * 2048 + (k + 1) * 128],
                                      xT[k][:, tc * 128:(tc + 1) * 128])
            wait("tensor", g_wvr)
            wait("tensor", xcol_loaded[cb])
            for j in range(4):
                wait("tensor", pS_free[j])
            g_vmm = None
            for k in range(NK):
                for j in range(4):
                    g_vmm = matmul(pS[:, j * 512:(j + 1) * 512],
                                   Bb[:, cb * 2048 + k * 128: cb * 2048 + (k + 1) * 128],
                                   A[:, k * 2048 + j * 512: k * 2048 + (j + 1) * 512],
                                   k == 0, k == NK - 1,
                                   then=("pe", 1) if (k == NK - 1 and j == 3) else None)
            xcol_free[cb] = g_vmm
            wait("vector", g_vmm)
            wait("vector", vstg_busy)
            g_vevac = None
            for j in range(4):
                g_vevac = emit("vector",
                               lambda e, j=j:
                               e.tensor_tensor(out=Bb[:, 9216 + j * 512: 9216 + (j + 1) * 512],
                                               in0=pS[:, j * 512:(j + 1) * 512],
                                               in1=t_bvb[:, j * 512:(j + 1) * 512],
                                               op=OP.add), ("dv", 1))
                pS_free[j] = g_vevac
            wait("scalar", g_vevac)
            for h in range(H):
                vstg_busy = dma("scalar", v_hbm[h][tc * 128:(tc + 1) * 128, :],
                                Bb[:, 9216 + h * 128: 9216 + (h + 1) * 128], sem="sp")

        V_DONE = gmax(xcol_free[0], xcol_free[1])
        wa_loaded = [None, None]
        wa_free = [None, None]
        stg_busy = [None, None]
        for qq in range(2):
            g_xq = None
            wait("sync", V_DONE)
            for k in range(NK):
                g_xq = dma("sync", A[:, qq * 8192 + k * 512: qq * 8192 + (k + 1) * 512],
                           xqf[k][:, qq * 512:(qq + 1) * 512])
            for h in range(H):
                buf = h % 2
                wait("sync", wa_free[buf])
                wa_loaded[buf] = dma("sync", Bb[:, buf * 2048:(buf + 1) * 2048],
                                     wq[h])
                sl = h % 2
                stg = h % 2
                wait("tensor", g_xq)
                wait("tensor", wa_loaded[buf])
                wait("tensor", pS_free[sl])
                g_qmm = None
                for k in range(NK):
                    g_qmm = matmul(pS[:, sl * 512:(sl + 1) * 512],
                                   Bb[:, buf * 2048 + k * 128: buf * 2048 + (k + 1) * 128],
                                   A[:, qq * 8192 + k * 512: qq * 8192 + (k + 1) * 512],
                                   k == 0, k == NK - 1,
                                   then=("pe", 1) if k == NK - 1 else None)
                wa_free[buf] = g_qmm
                wait("vector", g_qmm)
                wait("vector", stg_busy[stg])
                g_qev = emit("vector",
                             lambda e, sl=sl, stg=stg, bc=h:
                             e.tensor_scalar(out=Bb[:, 8192 + stg * 512: 8192 + (stg + 1) * 512],
                                             in0=pS[:, sl * 512:(sl + 1) * 512],
                                             scalar1=t_bqk[:, bc:bc + 1],
                                             scalar2=INV_ATTN, op0=OP.add, op1=OP.mult),
                             ("dv", 1))
                pS_free[sl] = g_qev
                wait("scalar", g_qev)
                stg_busy[stg] = dma("scalar", q_hbm[h][:, qq * 512:(qq + 1) * 512],
                                    Bb[:, 8192 + stg * 512: 8192 + (stg + 1) * 512],
                                    sem="sp")

        g_xt = [None] * NK
        for k in range(NK):
            wait("sync", gmax(wa_free[0], wa_free[1]))
            g_xt[k] = dma("sync", A[:, k * 2048:(k + 1) * 2048], xT[k])
        for h in range(H):
            buf = h % 2
            wait("sync", wa_free[buf])
            wa_loaded[buf] = dma("sync", Bb[:, buf * 2048:(buf + 1) * 2048],
                                 wk[h])
            for t in range(4):
                sl = (h * 4 + t) % 2
                stg = (h * 4 + t) % 2
                wait("tensor", wa_loaded[buf])
                wait("tensor", g_xt[NK - 1])
                wait("tensor", pS_free[sl])
                g_kmm = None
                for k in range(NK):
                    g_kmm = matmul(pS[:, sl * 512:(sl + 1) * 512],
                                   Bb[:, buf * 2048 + k * 128: buf * 2048 + (k + 1) * 128],
                                   A[:, k * 2048 + t * 512: k * 2048 + (t + 1) * 512],
                                   k == 0, k == NK - 1,
                                   then=("pe", 1) if k == NK - 1 else None)
                wa_free[buf] = g_kmm
                wait("vector", g_kmm)
                wait("vector", stg_busy[stg])
                g_kev = emit("vector",
                             lambda e, sl=sl, stg=stg, bc=H + h:
                             e.tensor_scalar(out=Bb[:, 8192 + stg * 512: 8192 + (stg + 1) * 512],
                                             in0=pS[:, sl * 512:(sl + 1) * 512],
                                             scalar1=t_bqk[:, bc:bc + 1],
                                             scalar2=None, op0=OP.add), ("dv", 1))
                pS_free[sl] = g_kev
                wait("scalar", g_kev)
                stg_busy[stg] = dma("scalar", k_hbm[h][:, t * 512:(t + 1) * 512],
                                    Bb[:, 8192 + stg * 512: 8192 + (stg + 1) * 512],
                                    sem="sp")

        if debug:
            wait("scalar", ("sp", cnt["sp"]))
            for h in range(H):
                dma("scalar", dbg_k[h], k_hbm[h], sem="sp")
                dma("scalar", dbg_v[h], v_hbm[h], sem="sp")
                dma("scalar", dbg_q[h], q_hbm[h], sem="sp")

        t_p = A[:, 0:2048]
        t_pt = A[:, 2048:4096]

        def t_z(h):
            return A[:, 4096 + h * 512: 4096 + (h + 1) * 512]

        def t_af(dk):
            return A[:, 12288 + dk * 512: 12288 + (dk + 1) * 512]

        def t_wo(b):
            return A[:, 20480 + b * 2048: 20480 + (b + 1) * 2048]

        t_tok0 = A[:, 24576:26624]
        t_tok1 = A[:, 26624:28672]
        t_x1s = Bb[:, 9216:11264]

        ks_loaded = [None, None]
        ks_free = [None, None]
        vs_loaded = [None, None]
        vs_free = [None, None]
        qg_loaded = [None, None]
        qg_free = [None, None]
        guard_sp_p1 = ("sp", cnt["sp"])
        wait("sync", guard_sp_p1)

        def p2_loads(g, h):
            hb = h % 2
            Lg = 1024 if g == 0 else 2048
            wait("sync", ks_free[hb])
            ks_loaded[hb] = dma("sync", Bb[:, hb * 2048: hb * 2048 + Lg],
                                k_hbm[h][:, 0:Lg])
            wait("sync", vs_free[hb])
            for j in range(Lg // 128):
                vs_loaded[hb] = dma("sync",
                                    Bb[:, 4096 + hb * 2048 + j * 128: 4096 + hb * 2048 + (j + 1) * 128],
                                    v_hbm[h][j * 128:(j + 1) * 128, :])
            wait("sync", qg_free[hb])
            qg_loaded[hb] = dma("sync", Bb[:, 8192 + hb * 512: 8192 + (hb + 1) * 512],
                                q_hbm[h][:, g * 512:(g + 1) * 512])

        x1_sp = None
        last_exp = None
        last_zev = None
        pt_bank_free = [None, None]
        pt_bank_mm = [None, None]
        for g in range(2):
            p2_loads(g, 0)
            stages = [(h, s) for h in range(H) for s in range(4)]
            g_exp = [None] * len(stages)
            g_negm = [None] * len(stages)
            g_scores = [None] * len(stages)
            g_pscale = [None] * len(stages)
            g_ptev = [None] * len(stages)
            g_zev = [None] * len(stages)

            def do_scores(i):
                h, s = stages[i]
                hb = h % 2
                slot = 4 * g + s
                Lbar = 256 * (slot + 1)
                wait("tensor", ks_loaded[hb])
                wait("tensor", qg_loaded[hb])
                if i > 0:
                    wait("tensor", g_exp[i - 1])
                elif g == 0:
                    for j in range(4):
                        wait("tensor", pS_free[j])
                else:
                    wait("tensor", last_exp)
                gg = None
                for jj in range((Lbar + 511) // 512):
                    w = min(512, Lbar - jj * 512)
                    gg = matmul(pS[:, jj * 512: jj * 512 + w],
                                Bb[:, 8192 + hb * 512 + s * 128: 8192 + hb * 512 + (s + 1) * 128],
                                Bb[:, hb * 2048 + jj * 512: hb * 2048 + jj * 512 + w],
                                True, True, then=("pe", 1))
                g_scores[i] = gg

            def do_softmax_front(i):
                h, s = stages[i]
                slot = 4 * g + s
                Lbar = 256 * (slot + 1)
                wait("vector", g_scores[i])
                emit("vector",
                     lambda e, sl=slot, L=Lbar:
                     e.tensor_tensor(out=pS[:, L - 256:L], in0=pS[:, L - 256:L],
                                     in1=t_mask[:, sl * 256:(sl + 1) * 256],
                                     op=OP.add), None)
                emit("vector",
                     lambda e, L=Lbar: e.reduce_max(out=t_st[:, 1:2], in_=pS[:, 0:L],
                                                    axis=AX.X, negate=True), None)
                g_negm[i] = emit("vector", lambda e: e.drain(), ("dv", 1))

            def do_exp(i):
                h, s = stages[i]
                Lbar = 256 * (4 * g + s + 1)
                wait("scalar", g_negm[i])
                g_exp[i] = emit("scalar",
                                lambda e, L=Lbar:
                                e.activation(out=t_p[:, 0:L], in_=pS[:, 0:L],
                                             func=ACT.Exp, bias=t_st[:, 1:2],
                                             scale=1.0, accum_out=t_st[:, 2:3]),
                                ("ac", 1))

            def do_pscale(i):
                h, s = stages[i]
                Lbar = 256 * (4 * g + s + 1)
                wait("vector", g_exp[i])
                emit("vector", lambda e: e.reciprocal(out=t_st[:, 3:4],
                                                      in_=t_st[:, 2:3]), None)
                g_rec = emit("vector", lambda e: e.drain(), ("dv", 1))
                wait("gpsimd", g_rec)
                g_pscale[i] = emit("gpsimd",
                                   lambda e, L=Lbar:
                                   e.tensor_scalar(out=t_p[:, 0:L], in0=t_p[:, 0:L],
                                                   scalar1=t_st[:, 3:4], scalar2=None,
                                                   op0=OP.mult), ("gp", 1))

            def do_pt_av(i):
                h, s = stages[i]
                hb = h % 2
                Lbar = 256 * (4 * g + s + 1)
                nj = Lbar // 128
                wait("tensor", g_pscale[i])
                last_ev = None
                for j0 in range(0, nj, 4):
                    tb = (j0 // 4) % 2
                    ptile = pT if tb == 0 else pT2
                    wait("tensor", pt_bank_free[tb])
                    for j in range(j0, min(j0 + 4, nj)):
                        pt_bank_mm[tb] = transpose(
                            ptile[:, (j - j0) * 128:(j - j0 + 1) * 128],
                            t_p[:, j * 128:(j + 1) * 128],
                            then=("pe", 1))
                    w = (min(j0 + 4, nj) - j0) * 128
                    wait("scalar", pt_bank_mm[tb])
                    last_ev = emit("scalar",
                                   lambda e, j0=j0, w=w, pt_=ptile:
                                   e.activation(out=t_pt[:, j0 * 128: j0 * 128 + w],
                                                in_=pt_[:, 0:w], func=ACT.Copy),
                                   ("ac", 1))
                    pt_bank_free[tb] = last_ev
                g_ptev[i] = last_ev
                wait("tensor", g_ptev[i])
                wait("tensor", vs_loaded[hb])
                if i > 0:
                    wait("tensor", g_zev[i - 1])
                elif last_zev is not None:
                    wait("tensor", last_zev)
                gg = None
                for j in range(nj):
                    gg = matmul(pZ[:, 0:128],
                                Bb[:, 4096 + hb * 2048 + j * 128: 4096 + hb * 2048 + (j + 1) * 128],
                                t_pt[:, j * 128:(j + 1) * 128],
                                j == 0, j == nj - 1,
                                then=("pe", 1) if j == nj - 1 else None)
                wait("vector", gg)
                g_zev[i] = emit("vector",
                                lambda e, h=h, s=s:
                                e.tensor_copy(t_z(h)[:, s * 128:(s + 1) * 128],
                                              pZ[:, 0:128]), ("dv", 1))

            for i in range(len(stages)):
                h, s = stages[i]
                if s == 0 and h + 1 < H:
                    p2_loads(g, h + 1)
                do_scores(i)
                do_softmax_front(i)
                do_exp(i)
                do_pscale(i)
                do_pt_av(i)
                if s == 3:
                    hb = h % 2
                    ks_free[hb] = g_scores[i]
                    qg_free[hb] = g_scores[i]
                    vs_free[hb] = ("pe", cnt["pe"])
            Z_READY = g_zev[len(stages) - 1]
            last_exp = g_exp[len(stages) - 1]
            last_zev = g_zev[len(stages) - 1]

            if debug:
                wait("scalar", Z_READY)
                dma("scalar", dbg_z[g], A[:, 4096:12288], sem="sp")
            wo_loaded = [None, None]
            wo_free = [None, None]
            g_af = None
            pb_free = [pt_bank_free[0], pt_bank_free[1]]
            for dk in range(NK):
                ob = dk % 2
                wait("sync", wo_free[ob])
                wo_loaded[ob] = dma("sync", t_wo(ob), wo[dk])
                pb = dk % 2
                ptile = pT if pb == 0 else pT2
                wait("tensor", wo_loaded[ob])
                wait("tensor", pb_free[pb])
                wait("tensor", Z_READY)
                gg = None
                for h in range(H):
                    gg = matmul(ptile[:, 0:512],
                                t_wo(ob)[:, h * 128:(h + 1) * 128],
                                t_z(h)[:, 0:512],
                                h == 0, h == H - 1,
                                then=("pe", 1) if h == H - 1 else None)
                wo_free[ob] = gg
                wait("vector", gg)
                g_af = emit("vector",
                            lambda e, dk=dk, pt_=ptile:
                            e.tensor_scalar(out=t_af(dk)[:, 0:512], in0=pt_[:, 0:512],
                                            scalar1=t_bo[:, dk:dk + 1],
                                            scalar2=None, op0=OP.add), ("dv", 1))
                pb_free[pb] = g_af

            if debug:
                wait("scalar", g_af)
                dma("scalar", dbg_af[g], A[:, 12288:20480], sem="sp")
            pt_bank_free[0] = pb_free[0]
            pt_bank_free[1] = pb_free[1]
            for s in range(4):
                slot = 4 * g + s
                g_trev = None
                for dk0 in range(0, NK, 4):
                    tb = (dk0 // 4) % 2
                    ptile = pT if tb == 0 else pT2
                    wait("tensor", pb_free[tb])
                    wait("tensor", g_af)
                    for dk in range(dk0, dk0 + 4):
                        pt_bank_mm[tb] = transpose(
                            ptile[:, (dk - dk0) * 128:(dk - dk0 + 1) * 128],
                            t_af(dk)[:, s * 128:(s + 1) * 128],
                            then=("pe", 1))
                    wait("vector", pt_bank_mm[tb])
                    g_trev = emit("vector",
                                  lambda e, dk0=dk0, pt_=ptile:
                                  e.tensor_copy(t_tok0[:, dk0 * 128:(dk0 + 4) * 128],
                                                pt_[:, 0:512]), ("dv", 1))
                    pb_free[tb] = g_trev
                wait("sync", x1_sp)
                g_xqr = dma("sync", t_tok1[:, :], xq[slot * 128:(slot + 1) * 128, :])
                wait("vector", g_xqr)
                wait("vector", g_trev)
                g_t0 = emit("vector", lambda e: e.tensor_tensor(out=t_tok0[:, :], in0=t_tok0[:, :],
                                                         in1=t_tok1[:, :], op=OP.add), ("dv", 1))
                if debug:
                    wait("scalar", g_t0)
                    g_dump = dma("scalar", dbg_t0[slot * 128:(slot + 1) * 128, :], t_tok0[:, :], sem="sp")
                    wait("vector", g_dump)
                lndump = None
                if debug and slot == 0:
                    def lndump(tag, ap_, gd, _dma=dma, _wait=wait):
                        _wait("scalar", gd)
                        gg = _dma("scalar", dbg_ln[tag], ap_, sem="sp")
                        _wait("vector", gg)
                        _wait("scalar", gg)
                g_x1 = ln_emit(emit, wait, t_tok0, t_tok1, t_st, 8, t_x1s, lndump)
                if debug:
                    wait("scalar", g_x1)
                    dma("scalar", dbg_st[slot], t_st[:, :], sem="sp")
                wait("scalar", g_x1)
                x1_sp = dma("scalar", x1t_hbm[slot * 128:(slot + 1) * 128, :],
                            t_tok1[:, :], sem="sp")
                wait("vector", x1_sp)
                g_fstage = None
                for k0 in range(0, NK, 4):
                    tb = (k0 // 4) % 2
                    ptile = pT if tb == 0 else pT2
                    wait("tensor", pb_free[tb])
                    wait("tensor", g_x1)
                    for k in range(k0, k0 + 4):
                        pt_bank_mm[tb] = transpose(
                            ptile[:, (k - k0) * 128:(k - k0 + 1) * 128],
                            t_tok1[:, k * 128:(k + 1) * 128],
                            then=("pe", 1))
                    wait("vector", pt_bank_mm[tb])
                    g_fstage = emit("vector",
                                    lambda e, k0=k0, pt_=ptile:
                                    e.tensor_copy(t_x1s[:, k0 * 128:(k0 + 4) * 128],
                                                  pt_[:, 0:512]), ("dv", 1))
                    pb_free[tb] = g_fstage
                wait("scalar", g_fstage)
                x1_sp = dma("scalar", x1f_hbm[slot], t_x1s[:, :], sem="sp")
            pt_bank_free[0] = pb_free[0]
            pt_bank_free[1] = pb_free[1]

        if debug:
            wait("scalar", ("sp", cnt["sp"]))
            dma("scalar", dbg_x1, x1t_hbm, sem="sp")

        def x1f(k):
            return A[:, k * 1024:(k + 1) * 1024]

        def t_h(ji):
            return A[:, 16384 + ji * 1024: 16384 + (ji + 1) * 1024]

        def t_wf(b):
            return Bb[:, b * 2048:(b + 1) * 2048]

        t_gs = Bb[:, 4096:5120]

        def t_ys(b):
            return Bb[:, 5120 + b * 512: 5120 + (b + 1) * 512]

        def t_w4(b):
            return C[:, b * 2048:(b + 1) * 2048]

        def t_sb(b):
            return C[:, 4096 + b * 2048: 4096 + (b + 1) * 2048]

        wait("sync", ("sp", cnt["sp"]))
        g_x1f = None
        for slot in range(NSLOT):
            g_x1f = dma("sync",
                        A[:, 0:16384].rearrange("p (k q) -> p k q", k=NK)[:, :, slot * 128:(slot + 1) * 128],
                        x1f_hbm[slot].rearrange("p (k j) -> p k j", k=NK))

        w4_loaded = [None, None]
        w4_free = [None, None]
        wf_free = [None, None]
        wf_ready = [None, None]
        gs_free = None
        ys_busy = [None, None]
        pd_free = [None, None]
        pS3_free = [None, None, None, None]
        h_consumed = None

        for bi, blk in enumerate(FFN_BLOCKS):
            nic_b = len(blk)
            for ji, ic in enumerate(blk):
                for gi, gate in enumerate((1, 0)):
                    osl = ic if gate else NIC + ic
                    wb = (2 * ji + gi) % 2
                    wait("sync", w4_free[wb])
                    v1 = dma("sync", t_w4(wb), w4gu[osl])
                    v2 = dma("sync", t_sb(wb), sbgu[osl])
                    deq = "gpsimd" if gate else "vector"
                    wait(deq, v1)
                    wait(deq, v2)
                    wait(deq, wf_free[wb])
                    gsem = "gp" if deq == "gpsimd" else "dv"
                    wf_ready[wb] = emit(deq,
                                        lambda e, wb=wb:
                                        e.tensor_tensor(out=t_wf(wb)[:, :],
                                                        in0=t_w4(wb)[:, :],
                                                        in1=t_sb(wb)[:, :],
                                                        op=OP.mult), (gsem, 1))
                    w4_free[wb] = wf_ready[wb]
                    wait("tensor", wf_ready[wb])
                    gg = None
                    for th in range(2):
                        sl = (0 if gate else 2) + th
                        wait("tensor", pS3_free[sl])
                        if bi == 0 and ji == 0 and gi == 0 and th == 0:
                            wait("tensor", g_x1f)
                            wait("tensor", last_exp)
                        for k in range(NK):
                            gg = matmul(pS[:, sl * 512:(sl + 1) * 512],
                                        t_wf(wb)[:, k * 128:(k + 1) * 128],
                                        x1f(k)[:, th * 512:(th + 1) * 512],
                                        k == 0, k == NK - 1,
                                        then=("pe", 1) if k == NK - 1 else None)
                    wf_free[wb] = gg
                    if gate:
                        GATE_PE = gg
                    else:
                        UP_PE = gg
                for th in range(2):
                    wait("scalar", GATE_PE)
                    wait("scalar", gs_free)
                    g_silu = emit("scalar",
                                  lambda e, th=th:
                                  e.activation(out=t_gs[:, th * 512:(th + 1) * 512],
                                               in_=pS[:, th * 512:(th + 1) * 512],
                                               func=ACT.Silu), ("ac", 1))
                    wait("vector", g_silu)
                    wait("vector", UP_PE)
                    if h_consumed is not None:
                        wait("vector", h_consumed)
                    g_h = emit("vector",
                               lambda e, ji=ji, th=th:
                               e.tensor_tensor(out=t_h(ji)[:, th * 512:(th + 1) * 512],
                                               in0=t_gs[:, th * 512:(th + 1) * 512],
                                               in1=pS[:, (2 + th) * 512:(3 + th) * 512],
                                               op=OP.mult), ("dv", 1))
                    pS3_free[th] = g_h
                    pS3_free[2 + th] = g_h
                gs_free = g_h
            h_consumed = None
            H_BLOCK = g_h
            g_down_last = None
            for osd in range(NK):
                wb = osd % 2
                wait("sync", w4_free[wb])
                c0, cw = blk[0] * 128, nic_b * 128
                v1 = dma("sync", t_w4(wb)[:, 0:cw], w4d[osd][:, c0:c0 + cw])
                v2 = dma("sync", t_sb(wb)[:, 0:cw], sbd[osd][:, c0:c0 + cw])
                deq = "gpsimd" if osd % 2 == 0 else "vector"
                wait(deq, v1)
                wait(deq, v2)
                wait(deq, wf_free[wb])
                gsem = "gp" if deq == "gpsimd" else "dv"
                wf_ready[wb] = emit(deq,
                                    lambda e, wb=wb, cw=cw:
                                    e.tensor_tensor(out=t_wf(wb)[:, 0:cw],
                                                    in0=t_w4(wb)[:, 0:cw],
                                                    in1=t_sb(wb)[:, 0:cw],
                                                    op=OP.mult), (gsem, 1))
                w4_free[wb] = wf_ready[wb]
                wait("tensor", wf_ready[wb])
                wait("tensor", H_BLOCK)
                for th in range(2):
                    pb = (2 * osd + th) % 2
                    pdt = pW if pb == 0 else pT
                    wait("tensor", pd_free[pb])
                    gg = None
                    for jc in range(nic_b):
                        gg = matmul(pdt[:, 0:512],
                                    t_wf(wb)[:, jc * 128:(jc + 1) * 128],
                                    t_h(jc)[:, th * 512:(th + 1) * 512],
                                    jc == 0, jc == nic_b - 1,
                                    then=("pe", 1) if jc == nic_b - 1 else None)
                    g_down_last = gg
                    ysl = (2 * osd + th) % 2
                    wait("vector", gg)
                    wait("vector", ys_busy[ysl])
                    g_ye = emit("vector",
                                lambda e, ysl=ysl, pdt=pdt:
                                e.tensor_copy(t_ys(ysl)[:, 0:512], pdt[:, 0:512]),
                                ("dv", 1))
                    pd_free[pb] = g_ye
                    wait("gpsimd", g_ye)
                    ys_busy[ysl] = dma("gpsimd", y_hbm[osd][:, th * 512:(th + 1) * 512],
                                       t_ys(ysl)[:, 0:512], sem="gpd",
                                       accum=(None if bi == 0 else OP.add))
                wf_free[wb] = g_down_last
            h_consumed = g_down_last

        if debug:
            wait("scalar", ("gpd", cnt["gpd"]))
            for dk in range(NK):
                dma("scalar", dbg_y[dk], y_hbm[dk], sem="sp")

        ystage = A[:, 16384:18432]
        tk0 = A[:, 18432:20480]
        tk1 = A[:, 20480:22528]
        wait("sync", ("gpd", cnt["gpd"]))
        pt_free_t = [pd_free[1], None]
        out_sp = None
        for slot in range(NSLOT):
            wait("sync", out_sp)
            g_y = None
            for dk in range(NK):
                g_y = dma("sync", ystage[:, dk * 128:(dk + 1) * 128],
                          y_hbm[dk][:, slot * 128:(slot + 1) * 128])
            g_x1r = dma("sync", tk1[:, :], x1t_hbm[slot * 128:(slot + 1) * 128, :])
            g_tr = None
            for dk0 in range(0, NK, 4):
                tb = (dk0 // 4) % 2
                ptile = pT if tb == 0 else pT2
                wait("tensor", g_y)
                wait("tensor", pt_free_t[tb])
                if dk0 == 0:
                    wait("vector", out_sp)
                for dk in range(dk0, dk0 + 4):
                    g_mm = transpose(ptile[:, (dk - dk0) * 128:(dk - dk0 + 1) * 128],
                                     ystage[:, dk * 128:(dk + 1) * 128],
                                     then=("pe", 1))
                wait("vector", g_mm)
                g_tr = emit("vector",
                            lambda e, dk0=dk0, pt_=ptile:
                            e.tensor_copy(tk0[:, dk0 * 128:(dk0 + 4) * 128],
                                          pt_[:, 0:512]), ("dv", 1))
                pt_free_t[tb] = g_tr
            if debug and slot == 0:
                wait("scalar", g_tr)
                gdd = dma("scalar", dbg_tail[0], tk0[:, :], sem="sp")
                wait("vector", gdd)
                wait("scalar", gdd)
            wait("scalar", g_tr)
            g_sq = emit("scalar",
                        lambda e: e.activation(out=ystage[:, 0:2048], in_=tk0[:, :],
                                               func=ACT.Square,
                                               accum_out=t_st[:, 16:17]), ("ac", 1))
            wait("vector", g_sq)
            emit("vector", lambda e: e.tensor_scalar(out=t_st[:, 17:18], in0=t_st[:, 16:17],
                                                     scalar1=INV_D, scalar2=float(RMS_EPS),
                                                     op0=OP.mult, op1=OP.add), None)
            g_v3 = emit("vector", lambda e: e.drain(), ("dv", 1))
            g_rs = rstd_core(emit, wait, t_st, 17, 18, g_v3)
            g_ym = emit("vector", lambda e: e.tensor_scalar(out=tk0[:, :], in0=tk0[:, :],
                                                     scalar1=t_st[:, 18:19], scalar2=None,
                                                     op0=OP.mult), ("dv", 1))
            if debug and slot == 0:
                wait("scalar", g_ym)
                gdd = dma("scalar", dbg_tail[1], tk0[:, :], sem="sp")
                wait("vector", gdd)
            wait("vector", g_x1r)
            g_t1 = emit("vector", lambda e: e.tensor_tensor(out=tk0[:, :], in0=tk0[:, :],
                                                     in1=tk1[:, :], op=OP.add), ("dv", 1))
            if debug and slot == 0:
                wait("scalar", g_t1)
                gdd = dma("scalar", dbg_tail[2], tk0[:, :], sem="sp")
                wait("vector", gdd)
            g_o = ln_emit(emit, wait, tk0, tk1, t_st, 24, ystage)
            wait("scalar", g_o)
            out_sp = dma("scalar", out[slot * 128:(slot + 1) * 128, :], tk1[:, :], sem="sp")

        wait("gpsimd", ("sp", cnt["sp"]))

        with nc.Block() as block:
            @block.sync
            def _(e):
                for fn in ops["sync"]:
                    fn(e)

            @block.tensor
            def _(e):
                for fn in ops["tensor"]:
                    fn(e)

            @block.vector
            def _(e):
                for fn in ops["vector"]:
                    fn(e)

            @block.scalar
            def _(e):
                for fn in ops["scalar"]:
                    fn(e)

            @block.gpsimd
            def _(e):
                for fn in ops["gpsimd"]:
                    fn(e)

    return nc


def ln_emit(emit, wait, src, dst, t_st, s0, scratch, lndump=None):
    def sd(fn):
        emit("vector", fn, None)
        return emit("vector", lambda e: e.drain(), ("dv", 1))

    sd(lambda e: e.reduce_sum(out=t_st[:, s0:s0 + 1], in_=src[:, :], axis=AX.X))
    sd(lambda e: e.tensor_scalar(out=t_st[:, s0 + 3:s0 + 4], in0=t_st[:, s0:s0 + 1],
                                 scalar1=INV_D, scalar2=None, op0=OP.mult))
    g_xc = emit("vector", lambda e: e.tensor_scalar(out=dst[:, :], in0=src[:, :],
                                                    scalar1=t_st[:, s0 + 3:s0 + 4],
                                                    scalar2=None, op0=OP.subtract),
                ("dv", 1))
    if lndump:
        lndump(0, dst[:, :], g_xc)
    wait("scalar", g_xc)
    g_sq = emit("scalar", lambda e: e.activation(out=scratch[:, 0:2048], in_=dst[:, :],
                                                 func=ACT.Square,
                                                 accum_out=t_st[:, s0 + 1:s0 + 2]),
                ("ac", 1))
    wait("vector", g_sq)
    g_v = sd(lambda e: e.tensor_scalar(out=t_st[:, s0 + 4:s0 + 5],
                                       in0=t_st[:, s0 + 1:s0 + 2],
                                       scalar1=INV_D, scalar2=float(LN_EPS),
                                       op0=OP.mult, op1=OP.add))
    g_r = rstd_core(emit, wait, t_st, s0 + 4, s0 + 2, g_v)
    if lndump:
        lndump(1, scratch[:, 0:2048], g_r)
        g_stt = emit("vector", lambda e: e.tensor_copy(scratch[:, 0:64], t_st[:, :]),
                     ("dv", 1))
        lndump(2, scratch[:, 0:2048], g_stt)
    return emit("vector", lambda e: e.tensor_scalar(out=dst[:, :], in0=dst[:, :],
                                                    scalar1=t_st[:, s0 + 2:s0 + 3],
                                                    scalar2=None, op0=OP.mult),
                ("dv", 1))


def rstd_core(emit, wait, t_st, sv, sr, g_v):
    def sd(fn):
        emit("vector", fn, None)
        return emit("vector", lambda e: e.drain(), ("dv", 1))

    wait("scalar", g_v)
    emit("scalar", lambda e: e.activation(out=t_st[:, 41:42], in_=t_st[:, sv:sv + 1],
                                          func=ACT.Ln), None)
    emit("scalar", lambda e: e.drain(), None)
    emit("scalar", lambda e: e.activation(out=t_st[:, 45:46], in_=t_st[:, 41:42],
                                          func=ACT.Exp, scale=-0.5), None)
    g_seed = emit("scalar", lambda e: e.drain(), ("ac", 1))
    wait("vector", g_seed)
    sd(lambda e: e.tensor_tensor(out=t_st[:, 42:43], in0=t_st[:, 45:46],
                                 in1=t_st[:, 45:46], op=OP.mult))
    sd(lambda e: e.tensor_tensor(out=t_st[:, 43:44], in0=t_st[:, 42:43],
                                 in1=t_st[:, sv:sv + 1], op=OP.mult))
    sd(lambda e: e.tensor_scalar(out=t_st[:, 44:45], in0=t_st[:, 43:44],
                                 scalar1=-0.5, scalar2=1.5,
                                 op0=OP.mult, op1=OP.add))
    return sd(lambda e: e.tensor_tensor(out=t_st[:, sr:sr + 1], in0=t_st[:, 45:46],
                                        in1=t_st[:, 44:45], op=OP.mult))


_F16_LUT = np.array([0.0, 0.5, 1.0, 1.5, 2.0, 3.0, 4.0, 6.0,
                     -0.0, -0.5, -1.0, -1.5, -2.0, -3.0, -4.0, -6.0],
                    dtype=np.float16)


def _fp4_to_f16(packed_i32):
    bts = np.ascontiguousarray(packed_i32.astype('<i4')).view(np.uint8)
    bts = bts.reshape(packed_i32.shape[0], packed_i32.shape[1] * 4)
    lo = bts & np.uint8(0x0F)
    hi = bts >> np.uint8(4)
    nib = np.empty((bts.shape[0], bts.shape[1] * 2), dtype=np.uint8)
    nib[:, 0::2] = lo
    nib[:, 1::2] = hi
    return _F16_LUT[nib]


def _prep(inputs):
    x = np.ascontiguousarray(np.asarray(inputs["L0_attn_x"], dtype=np.float32))
    W_Q = np.asarray(inputs["L0_attn_W_Q"], dtype=np.float32)
    W_K = np.asarray(inputs["L0_attn_W_K"], dtype=np.float32)
    W_V = np.asarray(inputs["L0_attn_W_V"], dtype=np.float32)
    W_O = np.asarray(inputs["L0_attn_W_O"], dtype=np.float32)
    b_Q = np.asarray(inputs["L0_attn_b_Q"], dtype=np.float32)
    b_K = np.asarray(inputs["L0_attn_b_K"], dtype=np.float32)
    b_V = np.asarray(inputs["L0_attn_b_V"], dtype=np.float32)
    b_O = np.asarray(inputs["L0_attn_b_O"], dtype=np.float32)

    shared = {}
    shared["wq"] = np.ascontiguousarray(
        W_Q.reshape(H, NK, 128, 128).transpose(0, 2, 1, 3).reshape(H, 128, NK * 128))
    shared["wk"] = np.ascontiguousarray(
        W_K.reshape(H, NK, 128, 128).transpose(0, 2, 1, 3).reshape(H, 128, NK * 128))
    shared["wvr"] = np.ascontiguousarray(
        W_V.reshape(H, NK, 128, 128).transpose(1, 2, 0, 3).reshape(NK, 128, D))
    shared["wo"] = np.ascontiguousarray(
        W_O.reshape(H, 128, NK, 128).transpose(2, 1, 0, 3).reshape(NK, 128, H * 128))
    shared["bqk"] = np.ascontiguousarray(np.concatenate([b_Q.T, b_K.T], axis=1))
    shared["bvb"] = np.ascontiguousarray(np.broadcast_to(b_V.reshape(1, D), (128, D)))
    shared["bo"] = np.ascontiguousarray(b_O.reshape(NK, 128).T)
    shared["identm"] = np.eye(128, dtype=np.float32)

    gu16 = _fp4_to_f16(np.asarray(inputs["L0_ffn_gate_up_packed"]))
    dn16 = _fp4_to_f16(np.asarray(inputs["L0_ffn_down_packed"]))
    gu_sc = np.asarray(inputs["L0_ffn_gate_up_scales"])
    dn_sc = np.asarray(inputs["L0_ffn_down_scales"])
    shared["w4gu"] = np.ascontiguousarray(
        gu16.reshape(2 * NIC, 128, NK, 128).transpose(0, 3, 2, 1)
        .reshape(2 * NIC, 128, NK * 128))
    sb1 = np.ascontiguousarray(gu_sc.T).reshape(2 * NIC, 128, NK)
    sb1 = sb1.transpose(0, 2, 1).reshape(2 * NIC, 1, NK * 128)
    shared["sbgu"] = np.ascontiguousarray(
        np.broadcast_to(sb1, (2 * NIC, 128, NK * 128)))
    shared["w4d"] = np.ascontiguousarray(
        dn16.reshape(NK, 128, NIC, 128).transpose(0, 3, 2, 1)
        .reshape(NK, 128, NIC * 128))
    sb2 = np.ascontiguousarray(dn_sc.T).reshape(NK, 128, NIC)
    sb2 = sb2.transpose(0, 2, 1).reshape(NK, 1, NIC * 128)
    shared["sbd"] = np.ascontiguousarray(np.broadcast_to(sb2, (NK, 128, NIC * 128)))

    in_maps = []
    for c in range(8):
        b, half = c // 2, c % 2
        m = dict(shared)
        xb = x[b]
        xTb = np.ascontiguousarray(xb.T)
        m["xT"] = np.ascontiguousarray(xTb.reshape(NK, 128, S))
        rows = np.concatenate([np.arange((2 * i + half) * 128, (2 * i + half) * 128 + 128)
                               for i in range(NSLOT)])
        m["xq"] = np.ascontiguousarray(xb[rows])
        m["xqf"] = np.ascontiguousarray(xTb[:, rows].reshape(NK, 128, QL))
        mask = np.zeros((128, NSLOT * 256), dtype=np.float32)
        for i in range(NSLOT):
            qc = 2 * i + half
            kcols = np.arange(2 * i * 128, (2 * i + 2) * 128)
            qrows = np.arange(qc * 128, (qc + 1) * 128)
            mask[:, i * 256:(i + 1) * 256] = np.where(
                kcols[None, :] > qrows[:, None], np.float32(MASK_NEG), np.float32(0.0))
        m["maskarr"] = mask
        in_maps.append(m)
    return in_maps


_PROG_CACHE = {}


def kernel(**inputs):
    in_maps = _prep(inputs)
    if "prog" not in _PROG_CACHE:
        _PROG_CACHE["prog"] = build_program(debug=0)
    nc = _PROG_CACHE["prog"]
    res = run_bass_kernel_spmd(nc, in_maps, core_ids=list(range(8)))
    full = np.empty((B, S, D), dtype=np.float32)
    for c in range(8):
        b, half = c // 2, c % 2
        o = np.asarray(res.results[c]["out"]).reshape(NSLOT, 128, D)
        for i in range(NSLOT):
            qc = 2 * i + half
            full[b, qc * 128:(qc + 1) * 128, :] = o[i]
    return full


# version 12
# speedup vs baseline: 1.4782x; 1.4782x over previous
import numpy as np
import concourse.bass as bass
import concourse.mybir as mybir
from concourse.bass_utils import run_bass_kernel_spmd
from concourse.alu_op_type import AluOpType as OP

F32 = mybir.dt.float32
F32R = mybir.dt.float32r
F16 = mybir.dt.float16
ACT = mybir.ActivationFunctionType
AX = mybir.AxisListType

B, S, D, H, E = 4, 2048, 2048, 16, 128
I = 5632
NK = D // 128
NSLOT = 8
QL = NSLOT * 128
NIC = I // 128
INV_ATTN = float(np.float32(1.0) / np.float32(np.sqrt(np.float32(E))))
MASK_NEG = -1.0e9
INV_D = float(np.float32(1.0) / np.float32(D))
LN_EPS = 1e-5
RMS_EPS = 1e-6

ENGINES = ("sync", "tensor", "vector", "scalar", "gpsimd")
FFN_BLOCKS = [list(range(0, 16)), list(range(16, 32)), list(range(32, 44))]


def build_program(debug=0):
    nc = bass.Bass("TRN2", target_bir_lowering=False)

    xT = nc.dram_tensor("xT", [NK, 128, S], F32R, kind="ExternalInput").ap()
    xqf = nc.dram_tensor("xqf", [NK, 128, QL], F32R, kind="ExternalInput").ap()
    xq = nc.dram_tensor("xq", [QL, D], F32R, kind="ExternalInput").ap()
    wq = nc.dram_tensor("wq", [H, 128, NK * 128], F32R, kind="ExternalInput").ap()
    wk = nc.dram_tensor("wk", [H, 128, NK * 128], F32R, kind="ExternalInput").ap()
    wvr = nc.dram_tensor("wvr", [NK, 128, D], F32R, kind="ExternalInput").ap()
    wo = nc.dram_tensor("wo", [NK, 128, H * 128], F32R, kind="ExternalInput").ap()
    bqk = nc.dram_tensor("bqk", [128, 2 * H], F32, kind="ExternalInput").ap()
    bvb = nc.dram_tensor("bvb", [128, D], F32, kind="ExternalInput").ap()
    bo = nc.dram_tensor("bo", [128, NK], F32, kind="ExternalInput").ap()
    maskarr = nc.dram_tensor("maskarr", [128, NSLOT * 256], F32, kind="ExternalInput").ap()
    identm = nc.dram_tensor("identm", [128, 128], F32R, kind="ExternalInput").ap()
    w4gu = nc.dram_tensor("w4gu", [2 * NIC, 128, NK * 128], F16, kind="ExternalInput").ap()
    sbgu = nc.dram_tensor("sbgu", [2 * NIC, 128, NK * 128], F16, kind="ExternalInput").ap()
    w4d = nc.dram_tensor("w4d", [NK, 128, NIC * 128], F16, kind="ExternalInput").ap()
    sbd = nc.dram_tensor("sbd", [NK, 128, NIC * 128], F16, kind="ExternalInput").ap()

    out = nc.dram_tensor("out", [QL, D], F32R, kind="ExternalOutput").ap()

    k_hbm = nc.dram_tensor("k_hbm", [H, 128, S], F32R).ap()
    v_hbm = nc.dram_tensor("v_hbm", [H, S, 128], F32R).ap()
    q_hbm = nc.dram_tensor("q_hbm", [H, 128, QL], F32R).ap()
    x1t_hbm = nc.dram_tensor("x1t_hbm", [QL, D], F32R).ap()
    x1f_hbm = nc.dram_tensor("x1f_hbm", [NSLOT, 128, NK * 128], F32R).ap()
    y_hbm = nc.dram_tensor("y_hbm", [NK, 128, QL], F32R).ap()

    dbg = {}
    if debug:
        dbg_k = nc.dram_tensor("dbg_k", [H, 128, S], F32R, kind="ExternalOutput").ap()
        dbg_v = nc.dram_tensor("dbg_v", [H, S, 128], F32R, kind="ExternalOutput").ap()
        dbg_q = nc.dram_tensor("dbg_q", [H, 128, QL], F32R, kind="ExternalOutput").ap()
        dbg_x1 = nc.dram_tensor("dbg_x1", [QL, D], F32R, kind="ExternalOutput").ap()
        dbg_z = nc.dram_tensor("dbg_z", [2, 128, H * 512], F32R, kind="ExternalOutput").ap()
        dbg_af = nc.dram_tensor("dbg_af", [2, 128, NK * 512], F32R, kind="ExternalOutput").ap()
        dbg_t0 = nc.dram_tensor("dbg_t0", [QL, D], F32R, kind="ExternalOutput").ap()
        dbg_st = nc.dram_tensor("dbg_st", [NSLOT, 128, 64], F32, kind="ExternalOutput").ap()
        dbg_ln = nc.dram_tensor("dbg_ln", [4, 128, D], F32R, kind="ExternalOutput").ap()
        dbg_tail = nc.dram_tensor("dbg_tail", [4, 128, D], F32R, kind="ExternalOutput").ap()
        dbg_y = nc.dram_tensor("dbg_y", [NK, 128, QL], F32R, kind="ExternalOutput").ap()

    ops = {e: [] for e in ENGINES}
    cnt = {}
    sems = {}

    def inc(sem, n):
        cnt[sem] = cnt.get(sem, 0) + n
        return cnt[sem]

    def emit(eng, fn, then=None):
        if then is None:
            ops[eng].append(fn)
            return None
        sem_name, n = then
        v = inc(sem_name, n)

        def wrapped(e, fn=fn, sem_name=sem_name, n=n):
            fn(e).then_inc(sems[sem_name], n)
        ops[eng].append(wrapped)
        return (sem_name, v)

    def wait(eng, guard):
        if guard is None:
            return
        sem, v = guard
        if v > 0:
            ops[eng].append(lambda e, s=sem, v=v: e.wait_ge(sems[s], v))

    def gmax(*guards):
        gs = [g for g in guards if g is not None and g[1] > 0]
        if not gs:
            return None
        sem = gs[0][0]
        assert all(g[0] == sem for g in gs), gs
        return (sem, max(g[1] for g in gs))

    def dma(qeng, dst, src, sem="ld", accum=None):
        if accum is None:
            return emit(qeng, lambda e, d=dst, s=src: e.dma_start(out=d, in_=s), (sem, 16))
        return emit(qeng, lambda e, d=dst, s=src, a=accum:
                    e.dma_start(out=d, in_=s, accum_op=a), (sem, 16))

    from contextlib import ExitStack
    es = ExitStack()
    with es:
        A = es.enter_context(nc.sbuf_tensor("A", [128, 32768], F32R))
        Bb = es.enter_context(nc.sbuf_tensor("Bb", [128, 11264], F32R))
        C = es.enter_context(nc.sbuf_tensor("C", [128, 8192], F16))
        t_ident = es.enter_context(nc.sbuf_tensor("t_ident", [128, 128], F32R))
        t_mask = es.enter_context(nc.sbuf_tensor("t_mask", [128, NSLOT * 256], F32))
        t_bqk = es.enter_context(nc.sbuf_tensor("t_bqk", [128, 2 * H], F32))
        t_bvb = es.enter_context(nc.sbuf_tensor("t_bvb", [128, D], F32))
        t_bo = es.enter_context(nc.sbuf_tensor("t_bo", [128, NK], F32))
        t_st = es.enter_context(nc.sbuf_tensor("t_st", [128, 64], F32))

        pS = es.enter_context(nc.psum_tensor("pS", [128, 2048], F32))
        pT = es.enter_context(nc.psum_tensor("pT", [128, 512], F32))
        pT2 = es.enter_context(nc.psum_tensor("pT2", [128, 512], F32))
        pZ = es.enter_context(nc.psum_tensor("pZ", [128, 512], F32))
        pW = es.enter_context(nc.psum_tensor("pW", [128, 512], F32))

        for s in ["ld", "sp", "gpd", "pe", "dv", "ac", "gp"]:
            sems[s] = es.enter_context(nc.semaphore(s))

        def matmul(out_ap, lhsT_ap, rhs_ap, first, last, then=None):
            return emit("tensor",
                        lambda e, o=out_ap, l=lhsT_ap, r=rhs_ap, f=first, la=last:
                        e.matmul(o, l.bitcast(F32R), r.bitcast(F32R), start=f, stop=la),
                        then)

        def transpose(out_ap, in_ap, then=None):
            return emit("tensor",
                        lambda e, o=out_ap, i=in_ap:
                        e.transpose(o.bitcast(F32R), i.bitcast(F32R), t_ident[:, :]),
                        then)

        g_const = None
        for dst, src in [(t_ident[:, :], identm), (t_mask[:, :], maskarr),
                         (t_bqk[:, :], bqk), (t_bvb[:, :], bvb), (t_bo[:, :], bo)]:
            g_const = dma("sync", dst, src)

        g_wvr = None
        for k in range(NK):
            g_wvr = dma("sync", A[:, k * 2048:(k + 1) * 2048], wvr[k])
        xcol_loaded = [None, None]
        xcol_free = [None, None]
        vstg_busy = None
        pS_free = [None, None, None, None]
        for tc in range(S // 128):
            cb = tc % 2
            wait("sync", xcol_free[cb])
            for k in range(NK):
                xcol_loaded[cb] = dma("sync",
                                      Bb[:, cb * 2048 + k * 128: cb * 2048 + (k + 1) * 128],
                                      xT[k][:, tc * 128:(tc + 1) * 128])
            wait("tensor", g_wvr)
            wait("tensor", xcol_loaded[cb])
            for j in range(4):
                wait("tensor", pS_free[j])
            g_vmm = None
            for k in range(NK):
                for j in range(4):
                    g_vmm = matmul(pS[:, j * 512:(j + 1) * 512],
                                   Bb[:, cb * 2048 + k * 128: cb * 2048 + (k + 1) * 128],
                                   A[:, k * 2048 + j * 512: k * 2048 + (j + 1) * 512],
                                   k == 0, k == NK - 1,
                                   then=("pe", 1) if (k == NK - 1 and j == 3) else None)
            xcol_free[cb] = g_vmm
            wait("vector", g_vmm)
            wait("vector", vstg_busy)
            g_vevac = None
            for j in range(4):
                g_vevac = emit("vector",
                               lambda e, j=j:
                               e.tensor_tensor(out=Bb[:, 9216 + j * 512: 9216 + (j + 1) * 512],
                                               in0=pS[:, j * 512:(j + 1) * 512],
                                               in1=t_bvb[:, j * 512:(j + 1) * 512],
                                               op=OP.add), ("dv", 1))
                pS_free[j] = g_vevac
            wait("scalar", g_vevac)
            for h in range(H):
                vstg_busy = dma("scalar", v_hbm[h][tc * 128:(tc + 1) * 128, :],
                                Bb[:, 9216 + h * 128: 9216 + (h + 1) * 128], sem="sp")

        V_DONE = gmax(xcol_free[0], xcol_free[1])
        wa_loaded = [None, None]
        wa_free = [None, None]
        stg_busy = [None, None]
        for qq in range(2):
            g_xq = None
            wait("sync", V_DONE)
            for k in range(NK):
                g_xq = dma("sync", A[:, qq * 8192 + k * 512: qq * 8192 + (k + 1) * 512],
                           xqf[k][:, qq * 512:(qq + 1) * 512])
            for h in range(H):
                buf = h % 2
                wait("sync", wa_free[buf])
                wa_loaded[buf] = dma("sync", Bb[:, buf * 2048:(buf + 1) * 2048],
                                     wq[h])
                sl = h % 2
                stg = h % 2
                wait("tensor", g_xq)
                wait("tensor", wa_loaded[buf])
                wait("tensor", pS_free[sl])
                g_qmm = None
                for k in range(NK):
                    g_qmm = matmul(pS[:, sl * 512:(sl + 1) * 512],
                                   Bb[:, buf * 2048 + k * 128: buf * 2048 + (k + 1) * 128],
                                   A[:, qq * 8192 + k * 512: qq * 8192 + (k + 1) * 512],
                                   k == 0, k == NK - 1,
                                   then=("pe", 1) if k == NK - 1 else None)
                wa_free[buf] = g_qmm
                wait("vector", g_qmm)
                wait("vector", stg_busy[stg])
                g_qev = emit("vector",
                             lambda e, sl=sl, stg=stg, bc=h:
                             e.tensor_scalar(out=Bb[:, 8192 + stg * 512: 8192 + (stg + 1) * 512],
                                             in0=pS[:, sl * 512:(sl + 1) * 512],
                                             scalar1=t_bqk[:, bc:bc + 1],
                                             scalar2=INV_ATTN, op0=OP.add, op1=OP.mult),
                             ("dv", 1))
                pS_free[sl] = g_qev
                wait("scalar", g_qev)
                stg_busy[stg] = dma("scalar", q_hbm[h][:, qq * 512:(qq + 1) * 512],
                                    Bb[:, 8192 + stg * 512: 8192 + (stg + 1) * 512],
                                    sem="sp")

        g_xt = [None] * NK
        for k in range(NK):
            wait("sync", gmax(wa_free[0], wa_free[1]))
            g_xt[k] = dma("sync", A[:, k * 2048:(k + 1) * 2048], xT[k])
        for h in range(H):
            buf = h % 2
            wait("sync", wa_free[buf])
            wa_loaded[buf] = dma("sync", Bb[:, buf * 2048:(buf + 1) * 2048],
                                 wk[h])
            for t in range(4):
                sl = (h * 4 + t) % 2
                stg = (h * 4 + t) % 2
                wait("tensor", wa_loaded[buf])
                wait("tensor", g_xt[NK - 1])
                wait("tensor", pS_free[sl])
                g_kmm = None
                for k in range(NK):
                    g_kmm = matmul(pS[:, sl * 512:(sl + 1) * 512],
                                   Bb[:, buf * 2048 + k * 128: buf * 2048 + (k + 1) * 128],
                                   A[:, k * 2048 + t * 512: k * 2048 + (t + 1) * 512],
                                   k == 0, k == NK - 1,
                                   then=("pe", 1) if k == NK - 1 else None)
                wa_free[buf] = g_kmm
                wait("vector", g_kmm)
                wait("vector", stg_busy[stg])
                g_kev = emit("vector",
                             lambda e, sl=sl, stg=stg, bc=H + h:
                             e.tensor_scalar(out=Bb[:, 8192 + stg * 512: 8192 + (stg + 1) * 512],
                                             in0=pS[:, sl * 512:(sl + 1) * 512],
                                             scalar1=t_bqk[:, bc:bc + 1],
                                             scalar2=None, op0=OP.add), ("dv", 1))
                pS_free[sl] = g_kev
                wait("scalar", g_kev)
                stg_busy[stg] = dma("scalar", k_hbm[h][:, t * 512:(t + 1) * 512],
                                    Bb[:, 8192 + stg * 512: 8192 + (stg + 1) * 512],
                                    sem="sp")

        if debug:
            wait("scalar", ("sp", cnt["sp"]))
            for h in range(H):
                dma("scalar", dbg_k[h], k_hbm[h], sem="sp")
                dma("scalar", dbg_v[h], v_hbm[h], sem="sp")
                dma("scalar", dbg_q[h], q_hbm[h], sem="sp")

        def t_p(par):
            return A[:, par * 2048:(par + 1) * 2048]

        def t_pt(par):
            return A[:, 4096 + par * 2048: 4096 + (par + 1) * 2048]

        def t_z(h):
            return A[:, 8192 + h * 512: 8192 + (h + 1) * 512]

        def t_af(dk):
            return A[:, 16384 + dk * 512: 16384 + (dk + 1) * 512]

        def t_wo(b):
            return A[:, 24576 + b * 2048: 24576 + (b + 1) * 2048]

        t_tok0 = A[:, 28672:30720]
        t_tok1 = A[:, 30720:32768]
        t_x1s = Bb[:, 9216:11264]

        ks_loaded = [None, None]
        ks_free = [None, None]
        vs_loaded = [None, None]
        vs_free = [None, None]
        qg_loaded = [None, None]
        qg_free = [None, None]
        guard_sp_p1 = ("sp", cnt["sp"])
        wait("sync", guard_sp_p1)

        def p2_loads(g, h):
            hb = h % 2
            Lg = 1024 if g == 0 else 2048
            wait("sync", ks_free[hb])
            ks_loaded[hb] = dma("sync", Bb[:, hb * 2048: hb * 2048 + Lg],
                                k_hbm[h][:, 0:Lg])
            wait("sync", vs_free[hb])
            for j in range(Lg // 128):
                vs_loaded[hb] = dma("sync",
                                    Bb[:, 4096 + hb * 2048 + j * 128: 4096 + hb * 2048 + (j + 1) * 128],
                                    v_hbm[h][j * 128:(j + 1) * 128, :])
            wait("sync", qg_free[hb])
            qg_loaded[hb] = dma("sync", Bb[:, 8192 + hb * 512: 8192 + (hb + 1) * 512],
                                q_hbm[h][:, g * 512:(g + 1) * 512])

        x1_sp = None
        last_exp = None
        pt_bank_free = [None, None]
        pw_free = None
        pz_free = None
        bank_guard = list(pS_free)
        prev = None
        zt_pend = None

        def emit_pt_av(st_p):
            nonlocal pz_free
            par, hb, nj = st_p["par"], st_p["hb"], st_p["nj"]
            last_ev = None
            for j0 in range(0, nj, 4):
                tb = (j0 // 4) % 2
                ptile = pT if tb == 0 else pT2
                wait("tensor", st_p["exp"])
                wait("tensor", pt_bank_free[tb])
                g_mm = None
                for j in range(j0, min(j0 + 4, nj)):
                    g_mm = transpose(ptile[:, (j - j0) * 128:(j - j0 + 1) * 128],
                                     t_p(par)[:, j * 128:(j + 1) * 128],
                                     then=("pe", 1))
                w = (min(j0 + 4, nj) - j0) * 128
                wait("scalar", g_mm)
                last_ev = emit("scalar",
                               lambda e, j0=j0, w=w, pt_=ptile, par=par:
                               e.activation(out=t_pt(par)[:, j0 * 128: j0 * 128 + w],
                                            in_=pt_[:, 0:w], func=ACT.Copy),
                               ("ac", 1))
                pt_bank_free[tb] = last_ev
            st_p["ptev"] = last_ev
            wait("tensor", st_p["ptev"])
            wait("tensor", vs_loaded[hb])
            wait("tensor", pz_free)
            gg = None
            for j in range(nj):
                gg = matmul(pZ[:, 0:128],
                            t_pt(par)[:, j * 128:(j + 1) * 128],
                            Bb[:, 4096 + hb * 2048 + j * 128: 4096 + hb * 2048 + (j + 1) * 128],
                            j == 0, j == nj - 1,
                            then=("pe", 1) if j == nj - 1 else None)
            st_p["av"] = gg
            if st_p["s"] == 3:
                vs_free[st_p["hb"]] = gg
            wait("vector", st_p["av"])
            wait("vector", st_p["recip"])
            pz_free = emit("vector",
                           lambda e, par=par:
                           e.tensor_scalar(out=t_p(par)[:, 0:128], in0=pZ[:, 0:128],
                                           scalar1=t_st[:, 4 + par:5 + par],
                                           scalar2=None, op0=OP.mult), ("dv", 1))
            st_p["zt_dv"] = pz_free

        def flush_zt(st_p):
            nonlocal pw_free
            wait("tensor", st_p["zt_dv"])
            wait("tensor", pw_free)
            g_mm = transpose(pW[:, 0:128], t_p(st_p["par"])[:, 0:128], then=("pe", 1))
            wait("scalar", g_mm)
            pw_free = emit("scalar",
                           lambda e, h=st_p["h"], s=st_p["s"]:
                           e.activation(out=t_z(h)[:, s * 128:(s + 1) * 128],
                                        in_=pW[:, 0:128], func=ACT.Copy), ("ac", 1))
            st_p["ztev"] = pw_free

        for g in range(2):
            p2_loads(g, 0)
            stages = [(h, s) for h in range(H) for s in range(4)]
            for i, (h, s) in enumerate(stages):
                if zt_pend is not None:
                    flush_zt(zt_pend)
                    zt_pend = None
                hb = h % 2
                par = i % 2
                slot = 4 * g + s
                Lbar = 256 * (slot + 1)
                nj = Lbar // 128
                nch = (Lbar + 511) // 512
                st_c = {"h": h, "s": s, "par": par, "hb": hb, "nj": nj}

                wait("tensor", ks_loaded[hb])
                wait("tensor", qg_loaded[hb])
                g_sc = [None] * nch
                for jj in range(nch):
                    w = min(512, Lbar - jj * 512)
                    wait("tensor", bank_guard[jj])
                    g_sc[jj] = matmul(pS[:, jj * 512: jj * 512 + w],
                                      Bb[:, 8192 + hb * 512 + s * 128: 8192 + hb * 512 + (s + 1) * 128],
                                      Bb[:, hb * 2048 + jj * 512: hb * 2048 + jj * 512 + w],
                                      True, True, then=("pe", 1))
                st_c["scores"] = g_sc[nch - 1]

                if prev is not None:
                    emit_pt_av(prev)
                    zt_pend = prev
                if s == 0 and h + 1 < H:
                    p2_loads(g, h + 1)

                for jj in range(nch):
                    w = min(512, Lbar - jj * 512)
                    wait("vector", g_sc[jj])
                    if jj == nch - 1:
                        emit("vector",
                             lambda e, sl=slot, L=Lbar:
                             e.tensor_tensor(out=pS[:, L - 256:L], in0=pS[:, L - 256:L],
                                             in1=t_mask[:, sl * 256:(sl + 1) * 256],
                                             op=OP.add), None)
                    bank_guard[jj] = emit("vector",
                                          lambda e, jj=jj, w=w, par=par:
                                          e.tensor_scalar(
                                              out=t_pt(par)[:, jj * 512: jj * 512 + w],
                                              in0=pS[:, jj * 512: jj * 512 + w],
                                              scalar1=1.0, scalar2=None,
                                              op0=OP.mult, op1=OP.max,
                                              accum_out=t_st[:, 8 + jj:9 + jj]),
                                          ("dv", 1))
                emit("vector", lambda e: e.drain(), None)
                emit("vector",
                     lambda e, nch=nch, par=par:
                     e.reduce_max(out=t_st[:, 6 + par:7 + par], in_=t_st[:, 8:8 + nch],
                                  axis=AX.X, negate=True), None)
                g_negm = emit("vector", lambda e: e.drain(), ("dv", 1))

                wait("scalar", g_negm)
                last_exp = emit("scalar",
                                lambda e, L=Lbar, par=par:
                                e.activation(out=t_p(par)[:, 0:L], in_=t_pt(par)[:, 0:L],
                                             func=ACT.Exp, bias=t_st[:, 6 + par:7 + par],
                                             scale=1.0, accum_out=t_st[:, 2 + par:3 + par]),
                                ("ac", 1))
                st_c["exp"] = last_exp

                wait("vector", st_c["exp"])
                emit("vector", lambda e, par=par: e.reciprocal(out=t_st[:, 4 + par:5 + par],
                                                               in_=t_st[:, 2 + par:3 + par]),
                     None)
                st_c["recip"] = emit("vector", lambda e: e.drain(), ("dv", 1))

                if s == 3:
                    ks_free[hb] = st_c["scores"]
                    qg_free[hb] = st_c["scores"]
                prev = st_c
            if zt_pend is not None:
                flush_zt(zt_pend)
                zt_pend = None
            emit_pt_av(prev)
            flush_zt(prev)
            prev = None
            Z_READY = pw_free

            wo_loaded = [None, None]
            wo_free = [None, None]
            g_af = None
            pb_free = [pt_bank_free[0], pt_bank_free[1]]
            for dk in range(NK):
                ob = dk % 2
                wait("sync", wo_free[ob])
                wo_loaded[ob] = dma("sync", t_wo(ob), wo[dk])
                pb = dk % 2
                ptile = pT if pb == 0 else pT2
                wait("tensor", wo_loaded[ob])
                wait("tensor", pb_free[pb])
                wait("tensor", Z_READY)
                gg = None
                for h in range(H):
                    gg = matmul(ptile[:, 0:512],
                                t_wo(ob)[:, h * 128:(h + 1) * 128],
                                t_z(h)[:, 0:512],
                                h == 0, h == H - 1,
                                then=("pe", 1) if h == H - 1 else None)
                wo_free[ob] = gg
                wait("vector", gg)
                g_af = emit("vector",
                            lambda e, dk=dk, pt_=ptile:
                            e.tensor_scalar(out=t_af(dk)[:, 0:512], in0=pt_[:, 0:512],
                                            scalar1=t_bo[:, dk:dk + 1],
                                            scalar2=None, op0=OP.add), ("dv", 1))
                pb_free[pb] = g_af

            if debug:
                wait("scalar", g_af)
                dma("scalar", dbg_af[g], A[:, 16384:24576], sem="sp")
            pt_bank_free[0] = pb_free[0]
            pt_bank_free[1] = pb_free[1]
            for s in range(4):
                slot = 4 * g + s
                g_trev = None
                for dk0 in range(0, NK, 4):
                    tb = (dk0 // 4) % 2
                    ptile = pT if tb == 0 else pT2
                    wait("tensor", pb_free[tb])
                    wait("tensor", g_af)
                    g_mm = None
                    for dk in range(dk0, dk0 + 4):
                        g_mm = transpose(
                            ptile[:, (dk - dk0) * 128:(dk - dk0 + 1) * 128],
                            t_af(dk)[:, s * 128:(s + 1) * 128],
                            then=("pe", 1))
                    wait("vector", g_mm)
                    g_trev = emit("vector",
                                  lambda e, dk0=dk0, pt_=ptile:
                                  e.tensor_copy(t_tok0[:, dk0 * 128:(dk0 + 4) * 128],
                                                pt_[:, 0:512]), ("dv", 1))
                    pb_free[tb] = g_trev
                wait("sync", x1_sp)
                g_xqr = dma("sync", t_tok1[:, :], xq[slot * 128:(slot + 1) * 128, :])
                wait("vector", g_xqr)
                wait("vector", g_trev)
                g_t0 = emit("vector", lambda e: e.tensor_tensor(out=t_tok0[:, :], in0=t_tok0[:, :],
                                                         in1=t_tok1[:, :], op=OP.add), ("dv", 1))
                if debug:
                    wait("scalar", g_t0)
                    g_dump = dma("scalar", dbg_t0[slot * 128:(slot + 1) * 128, :], t_tok0[:, :], sem="sp")
                    wait("vector", g_dump)
                lndump = None
                if debug and slot == 0:
                    def lndump(tag, ap_, gd, _dma=dma, _wait=wait):
                        _wait("scalar", gd)
                        gg = _dma("scalar", dbg_ln[tag], ap_, sem="sp")
                        _wait("vector", gg)
                        _wait("scalar", gg)
                g_x1 = ln_emit(emit, wait, t_tok0, t_tok1, t_st, 8, t_x1s, lndump)
                if debug:
                    wait("scalar", g_x1)
                    dma("scalar", dbg_st[slot], t_st[:, :], sem="sp")
                wait("scalar", g_x1)
                x1_sp = dma("scalar", x1t_hbm[slot * 128:(slot + 1) * 128, :],
                            t_tok1[:, :], sem="sp")
                wait("vector", x1_sp)
                g_fstage = None
                for k0 in range(0, NK, 4):
                    tb = (k0 // 4) % 2
                    ptile = pT if tb == 0 else pT2
                    wait("tensor", pb_free[tb])
                    wait("tensor", g_x1)
                    g_mm = None
                    for k in range(k0, k0 + 4):
                        g_mm = transpose(
                            ptile[:, (k - k0) * 128:(k - k0 + 1) * 128],
                            t_tok1[:, k * 128:(k + 1) * 128],
                            then=("pe", 1))
                    wait("vector", g_mm)
                    g_fstage = emit("vector",
                                    lambda e, k0=k0, pt_=ptile:
                                    e.tensor_copy(t_x1s[:, k0 * 128:(k0 + 4) * 128],
                                                  pt_[:, 0:512]), ("dv", 1))
                    pb_free[tb] = g_fstage
                wait("scalar", g_fstage)
                x1_sp = dma("scalar", x1f_hbm[slot], t_x1s[:, :], sem="sp")
            pt_bank_free[0] = pb_free[0]
            pt_bank_free[1] = pb_free[1]

        if debug:
            wait("scalar", ("sp", cnt["sp"]))
            dma("scalar", dbg_x1, x1t_hbm, sem="sp")

        def x1f(k):
            return A[:, k * 1024:(k + 1) * 1024]

        def t_h(ji):
            return A[:, 16384 + ji * 1024: 16384 + (ji + 1) * 1024]

        def t_wf(b):
            return Bb[:, b * 2048:(b + 1) * 2048]

        def t_gs(p):
            return Bb[:, 4096 + p * 2048: 4096 + p * 2048 + 1024]

        def t_ys(b):
            return Bb[:, 8192 + b * 512: 8192 + (b + 1) * 512]

        def t_w4(b):
            return C[:, b * 2048:(b + 1) * 2048]

        def t_sb(b):
            return C[:, 4096 + b * 2048: 4096 + (b + 1) * 2048]

        wait("sync", ("sp", cnt["sp"]))
        g_x1f = None
        for slot in range(NSLOT):
            g_x1f = dma("sync",
                        A[:, 0:16384].rearrange("p (k q) -> p k q", k=NK)[:, :, slot * 128:(slot + 1) * 128],
                        x1f_hbm[slot].rearrange("p (k j) -> p k j", k=NK))

        w4_loaded = [None, None]
        w4_free = [None, None]
        wf_free = [None, None]
        wf_ready = [None, None]
        gs_free = [None, None]
        ys_busy = [None, None]
        pd_free = [None, None]
        unit_free = [None] * 6
        h_consumed = None
        gcount = 0

        def pU(u):
            if u < 4:
                return pS[:, u * 512:(u + 1) * 512]
            return (pT2 if u == 4 else pZ)[:, 0:512]

        for bi, blk in enumerate(FFN_BLOCKS):
            nic_b = len(blk)
            for ji, ic in enumerate(blk):
                units = [(4 * gcount + m) % 6 for m in range(4)]
                gcount += 1
                for gi, gate in enumerate((1, 0)):
                    osl = ic if gate else NIC + ic
                    wb = (2 * ji + gi) % 2
                    wait("sync", w4_free[wb])
                    v1 = dma("sync", t_w4(wb), w4gu[osl])
                    v2 = dma("sync", t_sb(wb), sbgu[osl])
                    wait("vector", v1)
                    wait("vector", v2)
                    wait("vector", wf_free[wb])
                    wf_ready[wb] = emit("vector",
                                        lambda e, wb=wb:
                                        e.tensor_tensor(out=t_wf(wb)[:, :],
                                                        in0=t_w4(wb)[:, :],
                                                        in1=t_sb(wb)[:, :],
                                                        op=OP.mult), ("dv", 1))
                    w4_free[wb] = wf_ready[wb]
                    wait("tensor", wf_ready[wb])
                    gg = None
                    for th in range(2):
                        u = units[th] if gate else units[2 + th]
                        wait("tensor", unit_free[u])
                        if bi == 0 and ji == 0 and gi == 0 and th == 0:
                            wait("tensor", g_x1f)
                            wait("tensor", last_exp)
                        for k in range(NK):
                            gg = matmul(pU(u),
                                        t_wf(wb)[:, k * 128:(k + 1) * 128],
                                        x1f(k)[:, th * 512:(th + 1) * 512],
                                        k == 0, k == NK - 1,
                                        then=("pe", 1) if k == NK - 1 else None)
                    wf_free[wb] = gg
                    if gate:
                        GATE_PE = gg
                    else:
                        UP_PE = gg
                gp_ = ji % 2
                for th in range(2):
                    wait("scalar", GATE_PE)
                    wait("scalar", gs_free[gp_])
                    g_silu = emit("scalar",
                                  lambda e, th=th, gp_=gp_, u=units[th]:
                                  e.activation(out=t_gs(gp_)[:, th * 512:(th + 1) * 512],
                                               in_=pU(u),
                                               func=ACT.Silu), ("ac", 1))
                    unit_free[units[th]] = g_silu
                    wait("vector", g_silu)
                    wait("vector", UP_PE)
                    if h_consumed is not None:
                        wait("vector", h_consumed)
                    g_h = emit("vector",
                               lambda e, ji=ji, th=th, gp_=gp_, u=units[2 + th]:
                               e.tensor_tensor(out=t_h(ji)[:, th * 512:(th + 1) * 512],
                                               in0=t_gs(gp_)[:, th * 512:(th + 1) * 512],
                                               in1=pU(u),
                                               op=OP.mult), ("dv", 1))
                    unit_free[units[2 + th]] = g_h
                gs_free[gp_] = g_h
            h_consumed = None
            H_BLOCK = g_h
            g_down_last = None
            for osd in range(NK):
                wb = osd % 2
                wait("sync", w4_free[wb])
                c0, cw = blk[0] * 128, nic_b * 128
                v1 = dma("sync", t_w4(wb)[:, 0:cw], w4d[osd][:, c0:c0 + cw])
                v2 = dma("sync", t_sb(wb)[:, 0:cw], sbd[osd][:, c0:c0 + cw])
                deq = "vector"
                wait(deq, v1)
                wait(deq, v2)
                wait(deq, wf_free[wb])
                gsem = "gp" if deq == "gpsimd" else "dv"
                wf_ready[wb] = emit(deq,
                                    lambda e, wb=wb, cw=cw:
                                    e.tensor_tensor(out=t_wf(wb)[:, 0:cw],
                                                    in0=t_w4(wb)[:, 0:cw],
                                                    in1=t_sb(wb)[:, 0:cw],
                                                    op=OP.mult), (gsem, 1))
                w4_free[wb] = wf_ready[wb]
                wait("tensor", wf_ready[wb])
                wait("tensor", H_BLOCK)
                for th in range(2):
                    pb = (2 * osd + th) % 2
                    pdt = pW if pb == 0 else pT
                    wait("tensor", pd_free[pb])
                    gg = None
                    for jc in range(nic_b):
                        gg = matmul(pdt[:, 0:512],
                                    t_wf(wb)[:, jc * 128:(jc + 1) * 128],
                                    t_h(jc)[:, th * 512:(th + 1) * 512],
                                    jc == 0, jc == nic_b - 1,
                                    then=("pe", 1) if jc == nic_b - 1 else None)
                    g_down_last = gg
                    ysl = (2 * osd + th) % 2
                    wait("vector", gg)
                    wait("vector", ys_busy[ysl])
                    g_ye = emit("vector",
                                lambda e, ysl=ysl, pdt=pdt:
                                e.tensor_copy(t_ys(ysl)[:, 0:512], pdt[:, 0:512]),
                                ("dv", 1))
                    pd_free[pb] = g_ye
                    wait("gpsimd", g_ye)
                    ys_busy[ysl] = dma("gpsimd", y_hbm[osd][:, th * 512:(th + 1) * 512],
                                       t_ys(ysl)[:, 0:512], sem="gpd",
                                       accum=(None if bi == 0 else OP.add))
                wf_free[wb] = g_down_last
            h_consumed = g_down_last

        if debug:
            wait("scalar", ("gpd", cnt["gpd"]))
            for dk in range(NK):
                dma("scalar", dbg_y[dk], y_hbm[dk], sem="sp")

        def ystage_(p):
            return A[:, 16384 + p * 6144: 16384 + p * 6144 + 2048]

        def tk0_(p):
            return A[:, 18432 + p * 6144: 18432 + p * 6144 + 2048]

        def tk1_(p):
            return A[:, 20480 + p * 6144: 20480 + p * 6144 + 2048]

        wait("sync", ("gpd", cnt["gpd"]))
        pt_free_t = [pd_free[1], None]
        out_sp_p = [None, None]
        stat_base = [16, 32]
        for slot in range(NSLOT):
            par = slot % 2
            ystage = ystage_(par)
            tk0 = tk0_(par)
            tk1 = tk1_(par)
            sb = stat_base[par]
            wait("sync", out_sp_p[0])
            wait("sync", out_sp_p[1])
            wait("sync", ("dv", cnt["dv"]))
            g_y = None
            for dk in range(NK):
                g_y = dma("sync", ystage[:, dk * 128:(dk + 1) * 128],
                          y_hbm[dk][:, slot * 128:(slot + 1) * 128])
            g_x1r = dma("sync", tk1[:, :], x1t_hbm[slot * 128:(slot + 1) * 128, :])
            g_tr = None
            for dk0 in range(0, NK, 4):
                tb = (dk0 // 4) % 2
                ptile = pT if tb == 0 else pT2
                wait("tensor", g_y)
                wait("tensor", pt_free_t[tb])
                if dk0 == 0:
                    wait("vector", out_sp_p[par])
                g_mm = None
                for dk in range(dk0, dk0 + 4):
                    g_mm = transpose(ptile[:, (dk - dk0) * 128:(dk - dk0 + 1) * 128],
                                     ystage[:, dk * 128:(dk + 1) * 128],
                                     then=("pe", 1))
                wait("vector", g_mm)
                g_tr = emit("vector",
                            lambda e, dk0=dk0, pt_=ptile, tk0=tk0:
                            e.tensor_copy(tk0[:, dk0 * 128:(dk0 + 4) * 128],
                                          pt_[:, 0:512]), ("dv", 1))
                pt_free_t[tb] = g_tr
            if debug and slot == 3:
                wait("scalar", g_y)
                gdd = dma("scalar", dbg_tail[3], ystage[:, :], sem="sp")
                wait("scalar", g_tr)
                gdd = dma("scalar", dbg_tail[0], tk0[:, :], sem="sp")
                wait("vector", gdd)
                wait("scalar", gdd)
            wait("scalar", g_tr)
            g_sq = emit("scalar",
                        lambda e, ystage=ystage, tk0=tk0, sb=sb:
                        e.activation(out=ystage[:, 0:2048], in_=tk0[:, :],
                                     func=ACT.Square,
                                     accum_out=t_st[:, sb:sb + 1]), ("ac", 1))
            wait("vector", g_sq)
            emit("vector", lambda e, sb=sb: e.tensor_scalar(out=t_st[:, sb + 1:sb + 2],
                                                     in0=t_st[:, sb:sb + 1],
                                                     scalar1=INV_D, scalar2=float(RMS_EPS),
                                                     op0=OP.mult, op1=OP.add), None)
            g_v3 = emit("vector", lambda e: e.drain(), ("dv", 1))
            g_rs = rstd_core(emit, wait, t_st, sb + 1, sb + 2, g_v3)
            g_ym = emit("vector", lambda e, tk0=tk0, sb=sb:
                        e.tensor_scalar(out=tk0[:, :], in0=tk0[:, :],
                                        scalar1=t_st[:, sb + 2:sb + 3], scalar2=None,
                                        op0=OP.mult), ("dv", 1))
            if debug and slot == 3:
                wait("scalar", g_ym)
                gdd = dma("scalar", dbg_tail[1], tk0[:, :], sem="sp")
                wait("vector", gdd)
            wait("vector", g_x1r)
            g_t1 = emit("vector", lambda e, tk0=tk0, tk1=tk1:
                        e.tensor_tensor(out=tk0[:, :], in0=tk0[:, :],
                                        in1=tk1[:, :], op=OP.add), ("dv", 1))
            if debug and slot == 3:
                wait("scalar", g_t1)
                gdd = dma("scalar", dbg_tail[2], tk0[:, :], sem="sp")
                wait("vector", gdd)
            g_o = ln_emit(emit, wait, tk0, tk1, t_st, sb + 4, ystage)
            wait("scalar", g_o)
            out_sp = dma("scalar", out[slot * 128:(slot + 1) * 128, :], tk1[:, :], sem="sp")

        wait("gpsimd", ("sp", cnt["sp"]))

        with nc.Block() as block:
            @block.sync
            def _(e):
                for fn in ops["sync"]:
                    fn(e)

            @block.tensor
            def _(e):
                for fn in ops["tensor"]:
                    fn(e)

            @block.vector
            def _(e):
                for fn in ops["vector"]:
                    fn(e)

            @block.scalar
            def _(e):
                for fn in ops["scalar"]:
                    fn(e)

            @block.gpsimd
            def _(e):
                for fn in ops["gpsimd"]:
                    fn(e)

    return nc


def ln_emit(emit, wait, src, dst, t_st, s0, scratch, lndump=None):
    def sd(fn):
        emit("vector", fn, None)
        return emit("vector", lambda e: e.drain(), ("dv", 1))

    sd(lambda e: e.reduce_sum(out=t_st[:, s0:s0 + 1], in_=src[:, :], axis=AX.X))
    sd(lambda e: e.tensor_scalar(out=t_st[:, s0 + 3:s0 + 4], in0=t_st[:, s0:s0 + 1],
                                 scalar1=INV_D, scalar2=None, op0=OP.mult))
    g_xc = emit("vector", lambda e: e.tensor_scalar(out=dst[:, :], in0=src[:, :],
                                                    scalar1=t_st[:, s0 + 3:s0 + 4],
                                                    scalar2=None, op0=OP.subtract),
                ("dv", 1))
    if lndump:
        lndump(0, dst[:, :], g_xc)
    wait("scalar", g_xc)
    g_sq = emit("scalar", lambda e: e.activation(out=scratch[:, 0:2048], in_=dst[:, :],
                                                 func=ACT.Square,
                                                 accum_out=t_st[:, s0 + 1:s0 + 2]),
                ("ac", 1))
    wait("vector", g_sq)
    g_v = sd(lambda e: e.tensor_scalar(out=t_st[:, s0 + 4:s0 + 5],
                                       in0=t_st[:, s0 + 1:s0 + 2],
                                       scalar1=INV_D, scalar2=float(LN_EPS),
                                       op0=OP.mult, op1=OP.add))
    g_r = rstd_core(emit, wait, t_st, s0 + 4, s0 + 2, g_v)
    if lndump:
        lndump(1, scratch[:, 0:2048], g_r)
        g_stt = emit("vector", lambda e: e.tensor_copy(scratch[:, 0:64], t_st[:, :]),
                     ("dv", 1))
        lndump(2, scratch[:, 0:2048], g_stt)
    return emit("vector", lambda e: e.tensor_scalar(out=dst[:, :], in0=dst[:, :],
                                                    scalar1=t_st[:, s0 + 2:s0 + 3],
                                                    scalar2=None, op0=OP.mult),
                ("dv", 1))


def rstd_core(emit, wait, t_st, sv, sr, g_v):
    def sd(fn):
        emit("vector", fn, None)
        return emit("vector", lambda e: e.drain(), ("dv", 1))

    wait("scalar", g_v)
    emit("scalar", lambda e: e.activation(out=t_st[:, 41:42], in_=t_st[:, sv:sv + 1],
                                          func=ACT.Ln), None)
    emit("scalar", lambda e: e.drain(), None)
    emit("scalar", lambda e: e.activation(out=t_st[:, 45:46], in_=t_st[:, 41:42],
                                          func=ACT.Exp, scale=-0.5), None)
    g_seed = emit("scalar", lambda e: e.drain(), ("ac", 1))
    wait("vector", g_seed)
    sd(lambda e: e.tensor_tensor(out=t_st[:, 42:43], in0=t_st[:, 45:46],
                                 in1=t_st[:, 45:46], op=OP.mult))
    sd(lambda e: e.tensor_tensor(out=t_st[:, 43:44], in0=t_st[:, 42:43],
                                 in1=t_st[:, sv:sv + 1], op=OP.mult))
    sd(lambda e: e.tensor_scalar(out=t_st[:, 44:45], in0=t_st[:, 43:44],
                                 scalar1=-0.5, scalar2=1.5,
                                 op0=OP.mult, op1=OP.add))
    return sd(lambda e: e.tensor_tensor(out=t_st[:, sr:sr + 1], in0=t_st[:, 45:46],
                                        in1=t_st[:, 44:45], op=OP.mult))


_F16_LUT = np.array([0.0, 0.5, 1.0, 1.5, 2.0, 3.0, 4.0, 6.0,
                     -0.0, -0.5, -1.0, -1.5, -2.0, -3.0, -4.0, -6.0],
                    dtype=np.float16)


def _fp4_to_f16(packed_i32):
    bts = np.ascontiguousarray(packed_i32.astype('<i4')).view(np.uint8)
    bts = bts.reshape(packed_i32.shape[0], packed_i32.shape[1] * 4)
    lo = bts & np.uint8(0x0F)
    hi = bts >> np.uint8(4)
    nib = np.empty((bts.shape[0], bts.shape[1] * 2), dtype=np.uint8)
    nib[:, 0::2] = lo
    nib[:, 1::2] = hi
    return _F16_LUT[nib]


def _prep(inputs):
    x = np.ascontiguousarray(np.asarray(inputs["L0_attn_x"], dtype=np.float32))
    W_Q = np.asarray(inputs["L0_attn_W_Q"], dtype=np.float32)
    W_K = np.asarray(inputs["L0_attn_W_K"], dtype=np.float32)
    W_V = np.asarray(inputs["L0_attn_W_V"], dtype=np.float32)
    W_O = np.asarray(inputs["L0_attn_W_O"], dtype=np.float32)
    b_Q = np.asarray(inputs["L0_attn_b_Q"], dtype=np.float32)
    b_K = np.asarray(inputs["L0_attn_b_K"], dtype=np.float32)
    b_V = np.asarray(inputs["L0_attn_b_V"], dtype=np.float32)
    b_O = np.asarray(inputs["L0_attn_b_O"], dtype=np.float32)

    shared = {}
    shared["wq"] = np.ascontiguousarray(
        W_Q.reshape(H, NK, 128, 128).transpose(0, 2, 1, 3).reshape(H, 128, NK * 128))
    shared["wk"] = np.ascontiguousarray(
        W_K.reshape(H, NK, 128, 128).transpose(0, 2, 1, 3).reshape(H, 128, NK * 128))
    shared["wvr"] = np.ascontiguousarray(
        W_V.reshape(H, NK, 128, 128).transpose(1, 2, 0, 3).reshape(NK, 128, D))
    shared["wo"] = np.ascontiguousarray(
        W_O.reshape(H, 128, NK, 128).transpose(2, 1, 0, 3).reshape(NK, 128, H * 128))
    shared["bqk"] = np.ascontiguousarray(np.concatenate([b_Q.T, b_K.T], axis=1))
    shared["bvb"] = np.ascontiguousarray(np.broadcast_to(b_V.reshape(1, D), (128, D)))
    shared["bo"] = np.ascontiguousarray(b_O.reshape(NK, 128).T)
    shared["identm"] = np.eye(128, dtype=np.float32)

    gu16 = _fp4_to_f16(np.asarray(inputs["L0_ffn_gate_up_packed"]))
    dn16 = _fp4_to_f16(np.asarray(inputs["L0_ffn_down_packed"]))
    gu_sc = np.asarray(inputs["L0_ffn_gate_up_scales"])
    dn_sc = np.asarray(inputs["L0_ffn_down_scales"])
    shared["w4gu"] = np.ascontiguousarray(
        gu16.reshape(2 * NIC, 128, NK, 128).transpose(0, 3, 2, 1)
        .reshape(2 * NIC, 128, NK * 128))
    sb1 = np.ascontiguousarray(gu_sc.T).reshape(2 * NIC, 128, NK)
    sb1 = sb1.transpose(0, 2, 1).reshape(2 * NIC, 1, NK * 128)
    shared["sbgu"] = np.ascontiguousarray(
        np.broadcast_to(sb1, (2 * NIC, 128, NK * 128)))
    shared["w4d"] = np.ascontiguousarray(
        dn16.reshape(NK, 128, NIC, 128).transpose(0, 3, 2, 1)
        .reshape(NK, 128, NIC * 128))
    sb2 = np.ascontiguousarray(dn_sc.T).reshape(NK, 128, NIC)
    sb2 = sb2.transpose(0, 2, 1).reshape(NK, 1, NIC * 128)
    shared["sbd"] = np.ascontiguousarray(np.broadcast_to(sb2, (NK, 128, NIC * 128)))

    in_maps = []
    for c in range(8):
        b, half = c // 2, c % 2
        m = dict(shared)
        xb = x[b]
        xTb = np.ascontiguousarray(xb.T)
        m["xT"] = np.ascontiguousarray(xTb.reshape(NK, 128, S))
        rows = np.concatenate([np.arange((2 * i + half) * 128, (2 * i + half) * 128 + 128)
                               for i in range(NSLOT)])
        m["xq"] = np.ascontiguousarray(xb[rows])
        m["xqf"] = np.ascontiguousarray(xTb[:, rows].reshape(NK, 128, QL))
        mask = np.zeros((128, NSLOT * 256), dtype=np.float32)
        for i in range(NSLOT):
            qc = 2 * i + half
            kcols = np.arange(2 * i * 128, (2 * i + 2) * 128)
            qrows = np.arange(qc * 128, (qc + 1) * 128)
            mask[:, i * 256:(i + 1) * 256] = np.where(
                kcols[None, :] > qrows[:, None], np.float32(MASK_NEG), np.float32(0.0))
        m["maskarr"] = mask
        in_maps.append(m)
    return in_maps


_PROG_CACHE = {}


def kernel(**inputs):
    in_maps = _prep(inputs)
    if "prog" not in _PROG_CACHE:
        _PROG_CACHE["prog"] = build_program(debug=0)
    nc = _PROG_CACHE["prog"]
    res = run_bass_kernel_spmd(nc, in_maps, core_ids=list(range(8)))
    full = np.empty((B, S, D), dtype=np.float32)
    for c in range(8):
        b, half = c // 2, c % 2
        o = np.asarray(res.results[c]["out"]).reshape(NSLOT, 128, D)
        for i in range(NSLOT):
            qc = 2 * i + half
            full[b, qc * 128:(qc + 1) * 128, :] = o[i]
    return full
